# version 16
# baseline (speedup 1.0000x reference)
import contextlib
import os
import numpy as np
import ml_dtypes
import concourse.bass as bass
import concourse.mybir as mybir
from concourse.bass_utils import run_bass_kernel_spmd

F32 = mybir.dt.float32
BF16 = mybir.dt.bfloat16
F16 = mybir.dt.float16
AF = mybir.ActivationFunctionType
ALU = mybir.AluOpType
AX = mybir.AxisListType
BF = ml_dtypes.bfloat16

T = 4096
C = 1024
NC8 = 8
ARENA = 35800

ENGINES = ("sync", "vector", "scalar", "tensor", "gpsimd")
SEM_NAMES = ("LD", "WD", "ST", "V", "S", "TM", "EV", "GP")


class Sem:
    __slots__ = ("h", "v", "name")

    def __init__(self, h, name):
        self.h = h
        self.v = 0
        self.name = name


class Prog:

    AUTO = {"vector": "V", "scalar": "S"}

    def __init__(self, nc, sems, dt):
        self.nc = nc
        self.q = {e: [] for e in ENGINES}
        self.sems = sems
        self.dt = dt
        self.last = {e: None for e in ENGINES}

    def do(self, eng, fn, inc=None, n=1):
        if eng in self.AUTO:
            lv = self.last[eng]
            if lv is not None:
                self.q[eng].append(
                    lambda e, s=lv[0].h, v=lv[1]: e.wait_ge(s, v))
            if inc is None:
                inc = self.sems[self.AUTO[eng]]
                n = 1
        if inc is not None:
            inc.v += n
            def g(e, fn=fn, s=inc.h, n=n):
                fn(e).then_inc(s, n)
            self.q[eng].append(g)
            self.last[eng] = (inc, inc.v)
        else:
            self.q[eng].append(fn)

    def fence(self, eng):
        scr = self.dt["scr11"]
        if eng == "vector":
            self.do("vector", lambda e: e.memset(scr[0:1, 6:7], 0.0))
        else:
            self.do("scalar", lambda e: e.activation(out=scr[0:1, 7:8],
                                                     in_=scr[0:1, 7:8],
                                                     func=AF.Square))

    def tfence(self, bank_ap):
        p = self
        p.do("tensor", lambda e: e.matmul(
            out=bank_ap[0:1, 0:1], lhsT=p.dt["identb"][0:1, 1:2],
            rhs=p.dt["onesb"][0:1, 0:1], start=False, stop=True,
            skip_group_check=True), p.sems["TM"])

    def wait(self, eng, sem, v=None):
        if v is None:
            v = sem.v
        if v <= 0:
            return
        self.q[eng].append(lambda e, s=sem.h, v=v: e.wait_ge(s, v))

    def dma(self, eng, out, in_, inc):
        inc.v += 16
        def g(e, out=out, in_=in_, s=inc.h):
            e.dma_start(out=out, in_=in_).then_inc(s, 16)
        self.q[eng].append(g)

    def barrier(self, engines=ENGINES):
        s = self.sems
        scr = self.dt["scr11"]
        self.do("vector", lambda e: e.memset(scr[0:1, 0:1], 0.0), s["V"])
        self.do("scalar", lambda e: e.activation(out=scr[0:1, 1:2], in_=scr[0:1, 1:2],
                                                 func=AF.Square), s["S"])
        targets = [(s[n], s[n].v) for n in SEM_NAMES]
        for e in engines:
            for sem, v in targets:
                self.wait(e, sem, v)

    def emit(self, block):
        for e in ENGINES:
            ops = self.q[e]
            def runner(eng, ops=ops):
                for op in ops:
                    op(eng)
            getattr(block, e)(runner)


class Alloc:
    def __init__(self, arena_f32, total):
        self.a = arena_f32
        self.total = total
        self.off = 0
        self.peak = 0

    def _take(self, n_f32):
        ap = self.a[:, self.off:self.off + n_f32]
        self.off += n_f32
        self.peak = max(self.peak, self.off)
        assert self.off <= self.total, f"SBUF arena overflow {self.off}>{self.total}"
        return ap

    def f32(self, n):
        return self._take(n)

    def bf16(self, n):
        assert n % 2 == 0
        return self._take(n // 2).bitcast(BF16)

    def f16(self, n):
        assert n % 2 == 0
        return self._take(n // 2).bitcast(F16)

    def mark(self):
        return self.off

    def reset(self, off):
        self.off = off


class Rows:
    def __init__(self, ap, total):
        self.ap = ap
        self.total = total
        self.off = 0

    def take(self, n):
        r = self.ap[0:1, self.off:self.off + n]
        self.off += n
        assert self.off <= self.total
        return r


def slabify(w_co):
    CI, CO = w_co.shape
    nk, no = CI // 128, CO // 128
    return np.ascontiguousarray(
        w_co.reshape(nk, 128, no, 128).transpose(2, 1, 0, 3).reshape(no * 128, nk * 128))


def colmajor(vec):
    n = vec.shape[0]
    return np.ascontiguousarray(np.asarray(vec).reshape(n // 128, 128).T)


def emit_ln(p, al, xin, xh_tiles, Tw, col0):
    s = p.sems
    dt = p.dt
    LD, V, S, TM = s["LD"], s["V"], s["S"], s["TM"]
    pb, onesb = dt["pb"], dt["onesb"]
    ntt = Tw // 512

    base = al.mark()
    xstg = [al.f32(Tw), al.f32(Tw)]
    x2t = [al.bf16(512), al.bf16(512)]
    Mb = al.bf16(Tw)
    Rb = al.bf16(Tw)
    tmpc = al.bf16(Tw)
    s1row, s2row = dt["rowA"], dt["rowB"]
    msq = dt["rowC"]
    mrow_b, rrow_b = dt["rowbA"], dt["rowbB"]

    cast_v = []
    for c in range(NC8):
        if c >= 2:
            p.wait("sync", V, cast_v[c - 2])
        p.dma("sync", xstg[c % 2], xin[c * 128:(c + 1) * 128, col0:col0 + Tw], LD)
        p.wait("vector", LD, LD.v)
        p.do("vector", lambda e, o=xh_tiles[c], i=xstg[c % 2]:
             e.tensor_copy(out=o, in_=i), V)
        cast_v.append(V.v)

    evac_v = 0
    s2_tm = {}
    for tt in range(ntt):
        sl = slice(tt * 512, (tt + 1) * 512)
        for c in range(NC8):
            idx = tt * NC8 + c
            p.wait("scalar", V, cast_v[c])
            if idx >= 2:
                p.wait("scalar", TM, s2_tm[idx - 2])
            p.do("scalar", lambda e, o=x2t[idx % 2], i=xh_tiles[c], sl=sl:
                 e.activation(out=o, in_=i[:, sl], func=AF.Square), S)
            p.fence("scalar")
            s_now = S.v
            p.wait("tensor", V, cast_v[c])
            if c == 0 and tt > 0:
                p.wait("tensor", V, evac_v)
            p.do("tensor", lambda e, i=xh_tiles[c], sl=sl, st=(c == 0), sp=(c == NC8 - 1):
                 e.matmul(out=pb[0][0:1, :], lhsT=onesb[:, 0:1], rhs=i[:, sl],
                          start=st, stop=sp), TM if c == NC8 - 1 else None)
            p.wait("tensor", S, s_now)
            p.do("tensor", lambda e, i=x2t[idx % 2], st=(c == 0), sp=(c == NC8 - 1):
                 e.matmul(out=pb[1][0:1, :], lhsT=onesb[:, 0:1], rhs=i,
                          start=st, stop=sp), TM)
            s2_tm[idx] = TM.v
        p.tfence(pb[1][0:1, :])
        p.wait("vector", TM, TM.v)
        p.do("vector", lambda e, sl=sl: e.tensor_copy(out=s1row[0:1, sl], in_=pb[0][0:1, :]))
        p.do("vector", lambda e, sl=sl: e.tensor_copy(out=s2row[0:1, sl], in_=pb[1][0:1, :]), V)
        p.fence("vector")
        evac_v = V.v

    r = slice(0, Tw)
    p.do("vector", lambda e: e.tensor_scalar(out=s1row[0:1, r], in0=s1row[0:1, r],
         scalar1=1.0 / C, scalar2=None, op0=ALU.mult, op1=ALU.bypass))
    p.do("vector", lambda e: e.tensor_scalar(out=s2row[0:1, r], in0=s2row[0:1, r],
         scalar1=1.0 / C, scalar2=None, op0=ALU.mult, op1=ALU.bypass))
    p.do("vector", lambda e: e.tensor_tensor(out=msq[0:1, r], in0=s1row[0:1, r],
         in1=s1row[0:1, r], op=ALU.mult))
    p.do("vector", lambda e: e.tensor_tensor(out=msq[0:1, r], in0=s2row[0:1, r],
         in1=msq[0:1, r], op=ALU.subtract))
    p.do("vector", lambda e: e.tensor_copy(out=mrow_b[0:1, r], in_=s1row[0:1, r]), V)
    p.fence("vector")
    p.wait("scalar", V, V.v)
    p.do("scalar", lambda e: e.activation(out=msq[0:1, r], in_=msq[0:1, r],
         func=AF.Ln, bias=dt["fb"][0:1, 0:1], scale=1.0))
    p.do("scalar", lambda e: e.activation(out=rrow_b[0:1, r], in_=msq[0:1, r],
         func=AF.Exp, scale=-0.5), S)
    p.fence("scalar")

    p.wait("tensor", S, S.v)
    bc_v = 0
    for tt in range(ntt):
        sl = slice(tt * 512, (tt + 1) * 512)
        if tt > 0:
            p.wait("tensor", V, bc_v)
        p.do("tensor", lambda e, sl=sl: e.matmul(out=pb[2][:, :], lhsT=onesb[0:1, :],
             rhs=mrow_b[0:1, sl], start=True, stop=True))
        p.do("tensor", lambda e, sl=sl: e.matmul(out=pb[3][:, :], lhsT=onesb[0:1, :],
             rhs=rrow_b[0:1, sl], start=True, stop=True), TM)
        p.tfence(pb[3][0:1, :])
        p.wait("vector", TM, TM.v)
        p.do("vector", lambda e, sl=sl: e.tensor_copy(out=Mb[:, sl], in_=pb[2][:, :]))
        p.do("vector", lambda e, sl=sl: e.tensor_copy(out=Rb[:, sl], in_=pb[3][:, :]), V)
        bc_v = V.v

    for c in range(NC8):
        p.do("vector", lambda e, x=xh_tiles[c]: e.tensor_tensor(
            out=tmpc, in0=x, in1=Mb, op=ALU.subtract))
        p.do("vector", lambda e, x=xh_tiles[c]: e.tensor_tensor(
            out=x, in0=tmpc, in1=Rb, op=ALU.mult), V)
    p.fence("vector")
    al.reset(base)


def emit_linear(p, wslab_dram, nk, no, rhs_tiles, Tw, evac, rhs_ready=None,
                wdtype=F32, rhs_sl=None, group_barrier=False):
    s = p.sems
    dt = p.dt
    V, TM, EV, WD = s["V"], s["TM"], s["EV"], s["WD"]
    pb = dt["pb"]
    ls = dt["lin"]
    kw = nk * 128
    wstg, wsl = dt["wstg"], dt["wsl"]
    if rhs_sl is None:
        rhs_sl = slice(0, Tw)

    ntt = Tw // 512
    gb = min(4, ntt)
    ngr = ntt // gb
    for o in range(no):
        n = ls["n"]
        if wdtype == F32:
            if n >= 2:
                p.wait("sync", V, ls["cast"][n - 1])
            p.dma("sync", wstg[n % 2][:, 0:kw],
                  wslab_dram[o * 128:(o + 1) * 128, :], WD)
            p.wait("vector", WD, WD.v)
            if n >= 2:
                p.wait("vector", TM, ls["tm"][n - 2])
            p.do("vector", lambda e, d=wsl[n % 2][:, 0:kw], st=wstg[n % 2][:, 0:kw]:
                 e.tensor_copy(out=d, in_=st), V)
            p.fence("vector")
            ls["cast"][n] = V.v
            p.wait("tensor", V, ls["cast"][n])
        else:
            if n >= 2:
                p.wait("sync", TM, ls["tm"][n - 2])
            p.dma("sync", wsl[n % 2][:, 0:kw],
                  wslab_dram[o * 128:(o + 1) * 128, :], WD)
            ls["cast"][n] = s["V"].v
            p.wait("tensor", WD, WD.v)
        if rhs_ready is not None and o == 0:
            p.wait("tensor", rhs_ready[0], rhs_ready[1])
        for g in range(ngr):
            if ls["g"] >= 2:
                p.wait("tensor", EV, EV.v - 1)
            bank0 = (ls["g"] % 2) * 4
            banks = [pb[bank0 + i] for i in range(gb)]
            for k in range(nk):
                lhs = wsl[n % 2][:, k * 128:(k + 1) * 128]
                for i in range(gb):
                    tt = g * gb + i
                    sl = slice(rhs_sl.start + tt * 512, rhs_sl.start + (tt + 1) * 512)
                    p.do("tensor",
                         lambda e, b=banks[i], lhs=lhs, sl=sl, rt=rhs_tiles[k],
                         st=(k == 0), sp=(k == nk - 1):
                         e.matmul(out=b, lhsT=lhs, rhs=rt[:, sl], start=st, stop=sp),
                         TM if (k == nk - 1 and i == gb - 1) else None)
            p.tfence(banks[0])
            ls["tm"][n] = TM.v
            ev0 = EV.v
            evac(o, [(banks[i], g * gb + i) for i in range(gb)], p)
            assert EV.v == ev0 + 1, "evac must inc EV exactly once"
            ls["g"] += 1
            if group_barrier:
                p.barrier()
        ls["n"] += 1


def emit_stage_B(p, al, Din, W, Dout):
    s = p.sems
    dt = p.dt
    LD, V, S, TM, EV, ST = s["LD"], s["V"], s["S"], s["TM"], s["EV"], s["ST"]
    Tw = 1024
    base = al.mark()
    muR, muK = dt["muR"], dt["muK"]
    muRn, muKn = al.f32(NC8), al.f32(NC8)
    p.do("vector", lambda e: e.tensor_scalar(out=muRn, in0=muR, scalar1=-1.0,
         scalar2=1.0, op0=ALU.mult, op1=ALU.add))
    p.do("vector", lambda e: e.tensor_scalar(out=muKn, in0=muK, scalar1=-1.0,
         scalar2=1.0, op0=ALU.mult, op1=ALU.add), V)
    xh_last = al.f32(NC8)
    tmpb = al.bf16(Tw)

    wloop = al.mark()
    for w in range(T // Tw):
        al.reset(wloop)
        col0 = w * Tw
        rt = [al.f16(Tw) for _ in range(NC8)]
        k2 = [al.bf16(Tw) for _ in range(NC8)]
        xh = [al.bf16(Tw) for _ in range(NC8)]
        xr = [al.bf16(Tw) for _ in range(NC8)]
        xk = [al.bf16(Tw) for _ in range(NC8)]
        emit_ln(p, al, Din, xh, Tw, col0)
        for c in range(NC8):
            for (xo, mu, mun) in ((xr, muR, muRn), (xk, muK, muKn)):
                p.do("vector", lambda e, o=xo[c], i=xh[c], m=mu, c=c:
                     e.tensor_scalar(out=o, in0=i, scalar1=m[:, c:c + 1],
                                     scalar2=None, op0=ALU.mult, op1=ALU.bypass))
                p.do("vector", lambda e, i=xh[c], m=mun, c=c:
                     e.tensor_scalar(out=tmpb[:, 1:Tw], in0=i[:, 0:Tw - 1],
                                     scalar1=m[:, c:c + 1], scalar2=None,
                                     op0=ALU.mult, op1=ALU.bypass))
                p.do("vector", lambda e, o=xo[c]:
                     e.tensor_tensor(out=o[:, 1:Tw], in0=o[:, 1:Tw],
                                     in1=tmpb[:, 1:Tw], op=ALU.add))
                if w > 0:
                    p.do("vector", lambda e, m=mun, c=c:
                         e.tensor_scalar(out=tmpb[:, 0:1], in0=xh_last[:, c:c + 1],
                                         scalar1=m[:, c:c + 1], scalar2=None,
                                         op0=ALU.mult, op1=ALU.bypass))
                    p.do("vector", lambda e, o=xo[c]:
                         e.tensor_tensor(out=o[:, 0:1], in0=o[:, 0:1],
                                         in1=tmpb[:, 0:1], op=ALU.add))
        for c in range(NC8):
            p.do("vector", lambda e, c=c: e.tensor_copy(
                out=xh_last[:, c:c + 1], in_=xh[c][:, Tw - 1:Tw]))
        p.do("vector", lambda e: e.memset(dt["scr11"][0:1, 2:3], 0.0), V)
        vshift = V.v

        def evac_r(o, banks, p):
            for j, (b, tt) in enumerate(banks):
                sl = slice(tt * 512, (tt + 1) * 512)
                p.wait("scalar", TM, TM.v)
                p.do("scalar", lambda e, b=b, o=o, sl=sl:
                     e.activation(out=rt[o][:, sl], in_=b, func=AF.Sigmoid),
                     None if j < len(banks) - 1 else None)
            p.do("scalar", lambda e: e.activation(
                out=dt["scr11"][0:1, 7:8], in_=dt["scr11"][0:1, 7:8],
                func=AF.Square), EV)
        emit_linear(p, W["WrS"], NC8, NC8, xr, Tw, evac_r, rhs_ready=(V, vshift))

        def evac_k(o, banks, p):
            for j, (b, tt) in enumerate(banks):
                sl = slice(tt * 512, (tt + 1) * 512)
                p.wait("scalar", TM, TM.v)
                p.wait("scalar", EV, EV.v)
                p.do("scalar", lambda e, b=b: e.activation(
                    out=tmpb[:, 0:512], in_=b, func=AF.Relu))
                p.do("scalar", lambda e, o=o, sl=sl: e.activation(
                    out=k2[o][:, sl], in_=tmpb[:, 0:512], func=AF.Square))
            p.do("scalar", lambda e: e.activation(
                out=dt["scr11"][0:1, 7:8], in_=dt["scr11"][0:1, 7:8],
                func=AF.Square), EV)
        emit_linear(p, W["WkS"], NC8, NC8, xk, Tw, evac_k)

        xres = [al.f32(512) for _ in range(2)]
        outt = [al.f32(512) for _ in range(2)]
        tmpo = al.f32(512)
        state = {"n": 0}

        def evac_v(o, banks, p):
            for j, (b, tt) in enumerate(banks):
                sl = slice(col0 + tt * 512, col0 + (tt + 1) * 512)
                i = state["n"] % 2
                p.wait("gpsimd", TM, TM.v)
                if state["n"] >= 2:
                    p.wait("gpsimd", ST, ST.v)
                p.dma("gpsimd", xres[i], Din[o * 128:(o + 1) * 128, sl], LD)
                p.wait("vector", TM, TM.v)
                p.wait("vector", LD, LD.v)
                p.wait("vector", ST, ST.v)
                p.do("vector", lambda e, b=b, o=o, tt=tt: e.tensor_tensor(
                    out=tmpo, in0=b, in1=rt[o][:, tt * 512:(tt + 1) * 512],
                    op=ALU.mult))
                p.do("vector", lambda e, i=i: e.tensor_tensor(
                    out=outt[i], in0=tmpo, in1=xres[i], op=ALU.add),
                    EV if j == len(banks) - 1 else V)
                p.fence("vector")
                p.wait("gpsimd", EV if j == len(banks) - 1 else V)
                p.wait("gpsimd", V, V.v)
                p.dma("gpsimd", Dout[o * 128:(o + 1) * 128, sl], outt[i], ST)
                state["n"] += 1
        emit_linear(p, W["WvS"], NC8, NC8, k2, Tw, evac_v, group_barrier=True)
        p.barrier()
    al.reset(base)


def emit_weight_cache(p, al, src_f32, dst_bf16, n_slabs, kw):
    s = p.sems
    base = al.mark()
    stg = [al.f32(kw), al.f32(kw)]
    bfb = [al.bf16(kw), al.bf16(kw)]
    cast_hist, st_hist = {}, {}
    v0, st0 = s["V"].v, s["ST"].v
    for n in range(n_slabs):
        if n >= 2:
            p.wait("sync", s["V"], cast_hist[n - 2])
        else:
            p.wait("sync", s["V"], v0)
        p.dma("sync", stg[n % 2], src_f32[n * 128:(n + 1) * 128, :], s["WD"])
        p.wait("vector", s["WD"], s["WD"].v)
        p.wait("vector", s["ST"], st_hist[n - 2] if n >= 2 else st0)
        p.do("vector", lambda e, d=bfb[n % 2], st=stg[n % 2]:
             e.tensor_copy(out=d, in_=st), s["V"])
        cast_hist[n] = s["V"].v
        p.fence("vector")
        p.wait("gpsimd", s["V"], s["V"].v)
        p.dma("gpsimd", dst_bf16[n * 128:(n + 1) * 128, :], bfb[n % 2], s["ST"])
        st_hist[n] = s["ST"].v
    al.reset(base)


def emit_stage_D(p, al, Din, W, Dh, Dh2, Dout):
    s = p.sems
    dt = p.dt
    LD, V, S, TM, EV, ST = s["LD"], s["V"], s["S"], s["TM"], s["EV"], s["ST"]
    Tw = 1024
    emit_weight_cache(p, al, W["h1S"], W["h1B"], 32, 4096)
    emit_weight_cache(p, al, W["h2S"], W["h2B"], 32, 4096)
    emit_weight_cache(p, al, W["w3S"], W["w3B"], 8, 4096)
    p.barrier()

    base = al.mark()
    t1keep = al.bf16(Tw)
    tmpo = al.f32(512)
    for w in range(T // Tw):
        wbase = al.mark()
        col0 = w * Tw
        hbuf = [al.bf16(Tw) for _ in range(32)]
        xh = [al.bf16(Tw) for _ in range(NC8)]
        emit_ln(p, al, Din, xh, Tw, col0)
        p.barrier()
        if os.environ.get("D_DUMPX"):
            for c in range(NC8):
                p.dma("gpsimd", Dh[c * 128:(c + 1) * 128, col0:col0 + Tw], xh[c], ST)
            p.barrier()
            al.reset(wbase)
            continue

        def evac_w1(o, banks, p):
            p.wait("scalar", EV, EV.v)
            for j, (b, tt) in enumerate(banks):
                p.wait("scalar", TM, TM.v)
                p.do("scalar", lambda e, b=b, o=o, tt=tt: e.activation(
                    out=t1keep[:, tt * 512:(tt + 1) * 512], in_=b,
                    func=AF.Silu, scale=dt["s1"][:, o:o + 1]))
            p.do("scalar", lambda e: e.activation(
                out=dt["scr11"][0:1, 7:8], in_=dt["scr11"][0:1, 7:8],
                func=AF.Square), EV)

        def evac_w2(o, banks, p):
            for j, (b, tt) in enumerate(banks):
                sl = slice(tt * 512, (tt + 1) * 512)
                p.wait("vector", TM, TM.v)
                p.wait("vector", S, S.v)
                p.do("vector", lambda e, b=b, o=o: e.tensor_scalar(
                    out=tmpo, in0=b, scalar1=dt["s2"][:, o:o + 1],
                    scalar2=None, op0=ALU.mult, op1=ALU.bypass))
                p.do("vector", lambda e, o=o, sl=sl: e.tensor_tensor(
                    out=hbuf[o][:, sl], in0=tmpo, in1=t1keep[:, sl], op=ALU.mult),
                    EV if j == len(banks) - 1 else None)

        for o in range(32):
            def ev1(_, banks, p, o=o):
                evac_w1(o, banks, p)
            def ev2(_, banks, p, o=o):
                evac_w2(o, banks, p)
            emit_linear(p, W["w1S"][o * 128:(o + 1) * 128, :], NC8, 1, xh, Tw, ev1)
            emit_linear(p, W["w2S"][o * 128:(o + 1) * 128, :], NC8, 1, xh, Tw, ev2)
            if os.environ.get("D_SER"):
                p.barrier()
        p.barrier()
        if os.environ.get("D_DUMPA"):
            for o in range(32):
                p.dma("gpsimd", Dh[o * 128:(o + 1) * 128, col0:col0 + Tw],
                      hbuf[o], ST)
            p.barrier()
            al.reset(wbase)
            continue

        hOut = [al.bf16(512) for _ in range(2)]
        state2 = {"n": 0}

        def mk_evac_g(scol, dst):
            def evac_g(o, banks, p):
                for j, (b, tt) in enumerate(banks):
                    i = state2["n"] % 2
                    p.wait("scalar", TM, TM.v)
                    if state2["n"] >= 2:
                        p.wait("scalar", ST, ST.v)
                    p.do("scalar", lambda e, b=b, o=o, i=i: e.activation(
                        out=hOut[i], in_=b, func=AF.Gelu, scale=scol[:, o:o + 1]),
                        EV if j == len(banks) - 1 else S)
                    sem = EV if j == len(banks) - 1 else S
                    p.fence("scalar")
                    p.wait("gpsimd", sem, sem.v)
                    p.wait("gpsimd", S, S.v)
                    p.dma("gpsimd", dst[o * 128:(o + 1) * 128,
                                        col0 + tt * 512:col0 + (tt + 1) * 512],
                          hOut[i], ST)
                    state2["n"] += 1
            return evac_g

        emit_linear(p, W["h1B"], 32, 32, hbuf, Tw, mk_evac_g(dt["sh1"], Dh),
                    wdtype=BF16, group_barrier=True)
        p.barrier()

        for k in range(32):
            p.dma("sync", hbuf[k], Dh[k * 128:(k + 1) * 128, col0:col0 + Tw], LD)
        p.wait("tensor", LD, LD.v)
        emit_linear(p, W["h2B"], 32, 32, hbuf, Tw, mk_evac_g(dt["sh2"], Dh2),
                    wdtype=BF16, group_barrier=True)
        p.barrier()

        for k in range(32):
            p.dma("sync", hbuf[k], Dh2[k * 128:(k + 1) * 128, col0:col0 + Tw], LD)
        p.wait("tensor", LD, LD.v)
        xres = [al.f32(512) for _ in range(2)]
        outt = [al.f32(512) for _ in range(2)]
        state3 = {"n": 0}

        def evac_out(o, banks, p):
            for j, (b, tt) in enumerate(banks):
                sl = slice(col0 + tt * 512, col0 + (tt + 1) * 512)
                i = state3["n"] % 2
                p.wait("gpsimd", TM, TM.v)
                if state3["n"] >= 2:
                    p.wait("gpsimd", ST, ST.v)
                p.dma("gpsimd", xres[i], Din[o * 128:(o + 1) * 128, sl], LD)
                p.wait("vector", TM, TM.v)
                p.wait("vector", LD, LD.v)
                p.wait("vector", ST, ST.v)
                p.do("vector", lambda e, b=b, o=o: e.tensor_scalar(
                    out=tmpo, in0=b, scalar1=dt["s3"][:, o:o + 1],
                    scalar2=None, op0=ALU.mult, op1=ALU.bypass))
                p.do("vector", lambda e, i=i: e.tensor_tensor(
                    out=outt[i], in0=tmpo, in1=xres[i], op=ALU.add),
                    EV if j == len(banks) - 1 else V)
                p.fence("vector")
                p.wait("gpsimd", EV if j == len(banks) - 1 else V)
                p.wait("gpsimd", V, V.v)
                p.dma("gpsimd", Dout[o * 128:(o + 1) * 128, sl], outt[i], ST)
                state3["n"] += 1
        emit_linear(p, W["w3B"], 32, NC8, hbuf, Tw, evac_out, wdtype=BF16,
                    group_barrier=True)
        p.barrier()
        al.reset(wbase)
    al.reset(base)


def emit_stage_A(p, al, Din, W, D, Dout, heads=16, upto=99):
    s = p.sems
    dt = p.dt
    LD, V, S, TM, EV, ST = s["LD"], s["V"], s["S"], s["TM"], s["EV"], s["ST"]
    pb, onesb, identb = dt["pb"], dt["onesb"], dt["identb"]
    base = al.mark()

    xh = [al.bf16(T) for _ in range(NC8)]
    for w in range(2):
        sl_tiles = [x[:, w * 2048:(w + 1) * 2048] for x in xh]
        emit_ln(p, al, Din, sl_tiles, 2048, w * 2048)
    p.barrier()

    qb16 = al.f32(16)
    p.do("vector", lambda e: e.tensor_scalar(out=qb16, in0=dt["qkvB"][:, 0:16],
         scalar1=0.0625, scalar2=None, op0=ALU.mult, op1=ALU.bypass), V)
    ostg = [al.bf16(512) for _ in range(2)]
    stq = {"n": 0}

    def evac_qkv(o, banks, p):
        sc = 0.0625 if o < 16 else 1.0
        bcol = qb16[:, o:o + 1] if o < 16 else dt["qkvB"][:, o:o + 1]
        for j, (b, tt) in enumerate(banks):
            i = stq["n"] % 2
            p.wait("vector", TM, TM.v)
            if stq["n"] >= 2:
                p.wait("vector", ST, ST.v)
            p.do("vector", lambda e, b=b, i=i, sc=sc, bcol=bcol: e.tensor_scalar(
                out=ostg[i], in0=b, scalar1=sc, scalar2=bcol,
                op0=ALU.mult, op1=ALU.add), EV if j == len(banks) - 1 else V)
            sem = EV if j == len(banks) - 1 else V
            p.fence("vector")
            p.wait("gpsimd", sem, sem.v)
            p.wait("gpsimd", V, V.v)
            p.dma("gpsimd", D["qkT"][o * 128:(o + 1) * 128,
                                     tt * 512:(tt + 1) * 512], ostg[i], ST)
            stq["n"] += 1
    emit_linear(p, W["qkvS"], NC8, 24, xh, T, evac_qkv)
    p.barrier()
    al.reset(base)
    if upto <= 1:
        return

    chnk = [al.bf16(T), al.bf16(T)]
    ktstg = [al.bf16(128) for _ in range(2)]
    lacc = al.bf16(T)
    ltmq = al.bf16(T)
    pb6b = pb[6].bitcast(BF16)
    pb7b = pb[7].bitcast(BF16)
    dwcW, dwcB = dt["dwcW"], dt["dwcB"]
    ld_hist = {}
    tok_st = {}
    for reg in range(2):
        tokD = D["ktokD"] if reg == 0 else D["vtokD"]
        for c in range(NC8):
            n = reg * NC8 + c
            if n >= 2:
                p.wait("sync", V, ld_hist[n - 2][0])
                p.wait("sync", TM, ld_hist[n - 2][1])
            p.dma("sync", chnk[n % 2],
                  D["qkT"][(1 + reg) * 1024 + c * 128:(1 + reg) * 1024 + (c + 1) * 128, :], LD)
            p.wait("tensor", LD, LD.v)
            for j in range(32):
                kb = ktstg[j % 2]
                idx = n * 32 + j
                if idx >= 2:
                    p.wait("tensor", ST, tok_st[idx - 2])
                for hh in range(2):
                    pbt = pb6b if hh == 0 else pb7b
                    if j >= 1 or hh >= 1 or n >= 1:
                        p.wait("tensor", V, V.v)
                    p.do("tensor", lambda e, ch=chnk[n % 2], hh=hh, j=j, pbt=pbt:
                         e.transpose(out=pbt[:, 0:64],
                                     in_=ch[hh * 64:(hh + 1) * 64, j * 128:(j + 1) * 128],
                                     identity=identb[hh * 64:(hh + 1) * 64,
                                                     hh * 64:(hh + 1) * 64]), TM)
                    p.do("tensor", lambda e: e.matmul(
                        out=pb[5][0:1, 0:1], lhsT=identb[0:1, 1:2],
                        rhs=onesb[0:1, 0:1], start=False, stop=True,
                        skip_group_check=True), TM)
                    p.wait("vector", TM, TM.v)
                    p.do("vector", lambda e, kb=kb, hh=hh, pbt=pbt: e.tensor_copy(
                        out=kb[:, hh * 64:(hh + 1) * 64], in_=pbt[:, 0:64]), V)
                p.fence("vector")
                p.wait("gpsimd", V, V.v)
                p.dma("gpsimd", tokD[j * 128:(j + 1) * 128, c * 128:(c + 1) * 128],
                      kb, ST)
                tok_st[idx] = ST.v
            if reg == 1:
                p.wait("vector", S, S.v)
                p.wait("vector", ST, ST.v)
                p.do("vector", lambda e, ch=chnk[n % 2], c=c: e.tensor_scalar(
                    out=lacc, in0=ch, scalar1=dwcW[:, c * 3 + 1:c * 3 + 2],
                    scalar2=None, op0=ALU.mult, op1=ALU.bypass))
                p.do("vector", lambda e, ch=chnk[n % 2], c=c: e.tensor_scalar(
                    out=ltmq[:, 1:T], in0=ch[:, 0:T - 1],
                    scalar1=dwcW[:, c * 3:c * 3 + 1],
                    scalar2=None, op0=ALU.mult, op1=ALU.bypass))
                p.do("vector", lambda e: e.tensor_tensor(
                    out=lacc[:, 1:T], in0=lacc[:, 1:T], in1=ltmq[:, 1:T], op=ALU.add))
                p.do("vector", lambda e, ch=chnk[n % 2], c=c: e.tensor_scalar(
                    out=ltmq[:, 0:T - 1], in0=ch[:, 1:T],
                    scalar1=dwcW[:, c * 3 + 2:c * 3 + 3],
                    scalar2=None, op0=ALU.mult, op1=ALU.bypass))
                p.do("vector", lambda e: e.tensor_tensor(
                    out=lacc[:, 0:T - 1], in0=lacc[:, 0:T - 1], in1=ltmq[:, 0:T - 1],
                    op=ALU.add))
                p.do("vector", lambda e, c=c: e.tensor_scalar(
                    out=lacc, in0=lacc, scalar1=dwcB[:, c:c + 1], scalar2=None,
                    op0=ALU.add, op1=ALU.bypass), V)
                p.fence("vector")
                p.wait("gpsimd", V, V.v)
                p.dma("gpsimd", D["localD"][c * 128:(c + 1) * 128, :], lacc, ST)
            p.do("vector", lambda e: e.memset(dt["scr11"][0:1, 4:5], 0.0), V)
            ld_hist[n] = (V.v, TM.v)
    p.barrier()
    al.reset(base)
    if upto <= 2:
        return

    JB = 8
    kcolsF = al.f32(2048)
    skcol = al.f32(32)
    v65 = al.bf16(66 * 32)
    v65t = [v65[:, i * 66:i * 66 + 65] for i in range(32)]
    for i in range(32):
        p.do("vector", lambda e, i=i: e.memset(v65t[i][:, 64:65], 1.0),
             V if i == 31 else None)
    vones = V.v
    hbase = al.mark()
    prev_norm_v = 0
    for h in range(heads):
        al.reset(hbase)
        qmark = al.mark()
        qTh = al.bf16(T)
        p.wait("sync", V, prev_norm_v)
        p.dma("sync", qTh[0:64, :], D["qkT"][h * 64:(h + 1) * 64, :], LD)
        p.wait("tensor", LD, LD.v)
        p.wait("tensor", V, prev_norm_v)
        for tt in range(8):
            if tt > 0:
                p.wait("tensor", V, V.v)
            p.do("tensor", lambda e, tt=tt: e.matmul(
                out=pb[5][0:1, :], lhsT=onesb[0:64, 0:1],
                rhs=qTh[0:64, tt * 512:(tt + 1) * 512], start=True, stop=True), TM)
            p.tfence(pb[5][0:1, :])
            p.wait("vector", TM, TM.v)
            p.do("vector", lambda e, tt=tt: e.tensor_copy(
                out=dt["rowE"][0:1, tt * 512:(tt + 1) * 512], in_=pb[5][0:1, :]), V)
        p.fence("vector")
        p.wait("scalar", V, V.v)
        p.do("scalar", lambda e: e.activation(out=dt["rowbE"], in_=dt["rowE"],
             func=AF.Exp, scale=-1.0, bias=dt["fb"][0:1, 3:4]), S)
        ktb = al.bf16(64 * 32)
        for jg in range(32):
            p.dma("sync", ktb[:, jg * 64:(jg + 1) * 64],
                  D["ktokD"][jg * 128:(jg + 1) * 128, h * 64:(h + 1) * 64], LD)
            p.dma("sync", v65t[jg][:, 0:64],
                  D["vtokD"][jg * 128:(jg + 1) * 128, h * 64:(h + 1) * 64], LD)
        p.wait("vector", LD, LD.v)
        p.wait("vector", V, vones)
        p.do("vector", lambda e: e.tensor_copy(out=kcolsF, in_=ktb), V)
        p.fence("vector")
        for jg in range(32):
            p.do("vector", lambda e, jg=jg: e.tensor_reduce(
                out=skcol[:, jg:jg + 1], in_=kcolsF[:, jg * 64:(jg + 1) * 64],
                axis=AX.X, op=ALU.add), V if jg == 31 else None)
        vhead = V.v

        al.reset(qmark)
        amark = al.mark()
        accs = [al.f16(T) for _ in range(JB)]
        qdb = [al.bf16(T), al.bf16(T)]
        Td = al.f16(T)
        kern = [al.bf16(T)]
        attn_tm = {}
        exp_s = {}
        GP = s["GP"]
        Tdg = kern[0].bitcast(F16)
        NGP = 0
        gp_hist = {}
        for blk in range(32 // JB):
            jgs = list(range(blk * JB, (blk + 1) * JB))
            if blk == 0:
                p.wait("vector", V, vhead)
            else:
                p.wait("vector", S, S.v)
            for d in range(64):
                m = blk * 64 + d
                if m >= 2:
                    p.wait("sync", V, qdb_hist[m - 2])
                p.dma("sync", qdb[d % 2],
                      D["qkT"][h * 64 + d:h * 64 + d + 1, :].broadcast_to([128, T]), LD)
                p.wait("vector", LD, LD.v)

                def ts_prod(ji, jg, out):
                    p.do("vector", lambda e, out=out, jg=jg, d=d: e.tensor_scalar(
                        out=out, in0=qdb[d % 2],
                        scalar1=kcolsF[:, jg * 64 + d:jg * 64 + d + 1],
                        scalar2=None, op0=ALU.max, op1=ALU.bypass))

                if NGP == 0:
                    order = jgs
                else:
                    order = ([jgs[JB - NGP]] if d > 0 else []) + jgs[:JB - NGP] + \
                            (jgs[JB - NGP + 1:] if d > 0 else jgs[JB - NGP:])
                for jg in order:
                    ji = jg - blk * JB
                    if d == 0:
                        ts_prod(ji, jg, accs[ji])
                    elif ji >= JB - NGP:
                        p.wait("vector", GP, GP.v)
                        ts_prod(ji, jg, Tdg)
                        p.wait("gpsimd", V, V.v)
                        p.do("gpsimd", lambda e, ji=ji: e.tensor_tensor(
                            out=accs[ji], in0=accs[ji], in1=Tdg, op=ALU.add), GP)
                    else:
                        ts_prod(ji, jg, Td)
                        p.do("vector", lambda e, ji=ji: e.tensor_tensor(
                            out=accs[ji], in0=accs[ji], in1=Td, op=ALU.add))
                p.do("vector", lambda e: e.memset(dt["scr11"][0:1, 5:6], 0.0), V)
                if blk == 0 and d == 0:
                    qdb_hist = {}
                qdb_hist[m] = V.v
            p.fence("vector")
            vblk = V.v
            for ji, jg in enumerate(jgs):
                gi = 0
                p.wait("scalar", V, vblk)
                if ji == 0:
                    p.wait("scalar", GP, GP.v)
                if jg >= 1:
                    p.wait("scalar", TM, attn_tm[jg - 1])
                p.do("scalar", lambda e, ji=ji, gi=gi, jg=jg: e.activation(
                    out=kern[gi], in_=accs[ji], func=AF.Exp, scale=-2.0,
                    bias=skcol[:, jg:jg + 1]), S)
                p.fence("scalar")
                exp_s[jg] = S.v
                p.wait("tensor", S, exp_s[jg])
                for tt in range(8):
                    p.do("tensor", lambda e, gi=gi, tt=tt, jg=jg: e.matmul(
                        out=pb[tt][0:65, :], lhsT=v65t[jg],
                        rhs=kern[gi][:, tt * 512:(tt + 1) * 512],
                        start=(jg == 0), stop=(jg == 31)),
                        TM if tt == 7 else None)
                attn_tm[jg] = TM.v
        al.reset(amark)
        rb = al.bf16(T)
        locl = al.bf16(T)
        yh = al.bf16(T)
        p.wait("sync", TM, TM.v)
        p.dma("sync", locl[0:64, :], D["localD"][h * 64:(h + 1) * 64, :], LD)
        p.tfence(pb[0][0:1, :])
        p.wait("vector", TM, TM.v)
        for tt in range(8):
            p.do("vector", lambda e, tt=tt: e.tensor_copy(
                out=dt["rowE"][0:1, tt * 512:(tt + 1) * 512], in_=pb[tt][64:65, :]), V)
        p.do("vector", lambda e: e.tensor_tensor(
            out=dt["rowE"], in0=dt["rowE"], in1=dt["rowbE"], op=ALU.add), V)
        p.fence("vector")
        p.wait("scalar", V, V.v)
        p.do("scalar", lambda e: e.activation(out=dt["rowE"], in_=dt["rowE"],
             func=AF.Ln))
        p.do("scalar", lambda e: e.activation(out=dt["rowE"], in_=dt["rowE"],
             func=AF.Exp, scale=-1.0), S)
        p.fence("scalar")
        p.wait("vector", S, S.v)
        p.do("vector", lambda e: e.tensor_copy(out=dt["rowbE"], in_=dt["rowE"]), V)
        p.fence("vector")
        p.wait("gpsimd", V, V.v)
        p.dma("gpsimd", D["rbrow"], dt["rowbE"], ST)
        p.wait("sync", ST, ST.v)
        p.dma("sync", rb[0:64, :], D["rbrow"].broadcast_to([64, T]), LD)
        p.wait("vector", LD, LD.v)
        p.wait("vector", ST, ST.v)
        for tt in range(8):
            sl = slice(tt * 512, (tt + 1) * 512)
            p.do("vector", lambda e, tt=tt, sl=sl: e.tensor_tensor(
                out=yh[0:64, sl], in0=pb[tt][0:64, :], in1=rb[0:64, sl], op=ALU.mult))
        p.do("vector", lambda e: e.tensor_tensor(
            out=yh[0:64, :], in0=yh[0:64, :], in1=locl[0:64, :], op=ALU.add), V)
        prev_norm_v = V.v
        p.fence("vector")
        p.wait("gpsimd", V, V.v)
        p.dma("gpsimd", D["yTD"][h * 64:(h + 1) * 64, :], yh[0:64, :], ST)
    p.barrier()
    al.reset(base)
    if upto <= 3:
        return

    yt = [al.bf16(T) for _ in range(NC8)]
    for k in range(NC8):
        p.dma("sync", yt[k], D["yTD"][k * 128:(k + 1) * 128, :], LD)
    p.wait("tensor", LD, LD.v)
    xres = [al.f32(512) for _ in range(2)]
    outt = [al.f32(512) for _ in range(2)]
    tmpo = al.f32(512)
    stp = {"n": 0}

    def evac_proj(o, banks, p):
        for j, (b, tt) in enumerate(banks):
            sl = slice(tt * 512, (tt + 1) * 512)
            i = stp["n"] % 2
            p.wait("gpsimd", TM, TM.v)
            if stp["n"] >= 2:
                p.wait("gpsimd", ST, ST.v)
            p.dma("gpsimd", xres[i], Din[o * 128:(o + 1) * 128, sl], LD)
            p.wait("vector", TM, TM.v)
            p.wait("vector", LD, LD.v)
            p.do("vector", lambda e, b=b, o=o: e.tensor_scalar(
                out=tmpo, in0=b, scalar1=dt["projB"][:, o:o + 1], scalar2=None,
                op0=ALU.add, op1=ALU.bypass))
            p.do("vector", lambda e, i=i: e.tensor_tensor(
                out=outt[i], in0=tmpo, in1=xres[i], op=ALU.add),
                EV if j == len(banks) - 1 else V)
            p.fence("vector")
            p.wait("gpsimd", EV if j == len(banks) - 1 else V)
            p.wait("gpsimd", V, V.v)
            p.dma("gpsimd", Dout[o * 128:(o + 1) * 128, sl], outt[i], ST)
            stp["n"] += 1
    emit_linear(p, W["projS"], NC8, NC8, yt, T, evac_proj, group_barrier=True)
    p.barrier()
    al.reset(base)


def emit_stage_C(p, al, Din, W, D, Dout, upto=99):
    s = p.sems
    dt = p.dt
    LD, V, S, TM, EV, ST = s["LD"], s["V"], s["S"], s["TM"], s["EV"], s["ST"]
    DI = 2048
    base = al.mark()
    emit_weight_cache(p, al, W["WoutS"], W["WoutB"], 8, 2048)
    p.barrier()

    Tw = 2048
    for w in range(2):
        wb = al.mark()
        col0 = w * Tw
        xh = [al.bf16(Tw) for _ in range(NC8)]
        emit_ln(p, al, Din, xh, Tw, col0)
        ostg = [al.bf16(512) for _ in range(2)]
        st_ = {"n": 0}

        def evac_xz(o, banks, p):
            for j, (b, tt) in enumerate(banks):
                i = st_["n"] % 2
                p.wait("vector", TM, TM.v)
                if st_["n"] >= 2:
                    p.wait("vector", ST, ST.v)
                p.do("vector", lambda e, b=b, i=i: e.tensor_copy(out=ostg[i], in_=b),
                     EV if j == len(banks) - 1 else V)
                sem = EV if j == len(banks) - 1 else V
                p.fence("vector")
                p.wait("gpsimd", sem, sem.v)
                p.wait("gpsimd", V, V.v)
                p.dma("gpsimd", D["xz"][o * 128:(o + 1) * 128,
                                        col0 + tt * 512:col0 + (tt + 1) * 512],
                      ostg[i], ST)
                st_["n"] += 1
        emit_linear(p, W["WinS"], NC8, 32, xh, Tw, evac_xz)
        al.reset(wb)
        p.barrier()
    if upto <= 1:
        al.reset(base)
        return

    c2mark = al.mark()
    pb = dt["pb"]
    wxb = al.bf16(512)
    wxs = al.f32(512)
    p.dma("sync", wxs, W["WxP"], s["WD"])
    p.wait("vector", s["WD"], s["WD"].v)
    p.do("vector", lambda e: e.tensor_copy(out=wxb, in_=wxs), V)
    p.fence("vector")
    vwx = V.v
    p.wait("tensor", V, vwx)

    xpc = [al.bf16(T) for _ in range(2)]
    acc = al.bf16(T)
    tmq = al.bf16(T)
    xcv = al.bf16(T)
    convW, convB, Dvec = dt["convW"], dt["convB"], dt["Dvec"]
    ld_hist = {}
    for dc in range(16):
        if dc >= 2:
            p.wait("sync", V, ld_hist[dc - 2])
        p.dma("sync", xpc[dc % 2], D["xz"][dc * 128:(dc + 1) * 128, :], LD)
        p.wait("vector", LD, LD.v)
        if dc > 0:
            p.wait("vector", S, S.v)
        p.do("vector", lambda e, x=xpc[dc % 2], dc=dc: e.tensor_scalar(
            out=acc, in0=x, scalar1=convW[:, dc * 4 + 3:dc * 4 + 4],
            scalar2=None, op0=ALU.mult, op1=ALU.bypass))
        for rr in range(3):
            sh = 3 - rr
            p.do("vector", lambda e, x=xpc[dc % 2], dc=dc, rr=rr, sh=sh:
                 e.tensor_scalar(out=tmq[:, sh:T], in0=x[:, 0:T - sh],
                                 scalar1=convW[:, dc * 4 + rr:dc * 4 + rr + 1],
                                 scalar2=None, op0=ALU.mult, op1=ALU.bypass))
            p.do("vector", lambda e, sh=sh: e.tensor_tensor(
                out=acc[:, sh:T], in0=acc[:, sh:T], in1=tmq[:, sh:T], op=ALU.add))
        p.do("vector", lambda e: e.memset(dt["scr11"][0:1, 3:4], 0.0), V)
        ld_hist[dc] = V.v
        p.wait("scalar", V, V.v)
        p.do("scalar", lambda e, dc=dc: e.activation(
            out=xcv, in_=acc, func=AF.Silu, bias=convB[:, dc:dc + 1]), S)
        p.fence("scalar")
        p.fence("scalar")
        p.wait("gpsimd", S, S.v)
        p.dma("gpsimd", D["xconv"][dc * 128:(dc + 1) * 128, :], xcv, ST)
        p.wait("tensor", S, S.v)
        for tt in range(8):
            p.do("tensor", lambda e, tt=tt, dc=dc: e.matmul(
                out=pb[tt][0:32, :], lhsT=wxb[:, dc * 32:(dc + 1) * 32],
                rhs=xcv[:, tt * 512:(tt + 1) * 512],
                start=(dc == 0), stop=(dc == 15)),
                TM if tt == 7 else None)
        if dc < 15:
            p.wait("scalar", TM, TM.v)
            p.wait("scalar", ST, ST.v)
    p.tfence(pb[7][0:1, :])
    p.wait("vector", TM, TM.v)
    xssm = al.f32(T)
    xssmb = al.bf16(T)
    for tt in range(8):
        p.do("vector", lambda e, tt=tt: e.tensor_copy(
            out=xssm[0:32, tt * 512:(tt + 1) * 512], in_=pb[tt][0:32, :]))
    p.do("vector", lambda e: e.tensor_copy(out=xssmb[0:32, :], in_=xssm[0:32, :]), V)
    p.fence("vector")
    p.fence("vector")
    p.wait("gpsimd", V, V.v)
    p.dma("gpsimd", D["xssmb"], xssmb[0:32, :], ST)
    p.barrier()
    if upto <= 2:
        al.reset(base)
        return

    wdts = al.f32(2048)
    wdtb = al.bf16(2048)
    p.dma("sync", wdts[0:16, :], W["WdtP"], s["WD"])
    p.wait("vector", s["WD"], s["WD"].v)
    p.do("vector", lambda e: e.tensor_copy(out=wdtb[0:16, :], in_=wdts[0:16, :]), V)
    p.fence("vector")
    p.wait("tensor", V, V.v)
    dtt = al.bf16(512)
    trl = al.bf16(512)
    ta = al.bf16(512)
    sb4 = al.f32(512)
    bdt = dt["bdt"]
    ev_hist = {}
    for o in range(16):
        for tt in range(8):
            idx = o * 8 + tt
            bank = pb[idx % 4]
            if idx >= 4:
                p.wait("tensor", EV, ev_hist[idx - 4])
            p.do("tensor", lambda e, bank=bank, o=o, tt=tt: e.matmul(
                out=bank[:, :], lhsT=wdtb[0:16, o * 128:(o + 1) * 128],
                rhs=xssmb[0:16, tt * 512:(tt + 1) * 512],
                start=True, stop=True), TM)
            p.tfence(bank[0:1, :])
            p.wait("vector", TM, TM.v)
            p.wait("vector", S, S.v)
            p.wait("vector", ST, ST.v)
            p.do("vector", lambda e, bank=bank: e.tensor_copy(out=sb4, in_=bank))
            p.do("vector", lambda e, o=o: e.tensor_scalar(
                out=trl, in0=sb4, scalar1=bdt[:, o:o + 1], scalar2=0.0,
                op0=ALU.add, op1=ALU.max), V)
            ev_hist[idx] = EV.v + 1
            p.wait("scalar", V, V.v)
            p.do("scalar", lambda e, o=o: e.activation(
                out=ta, in_=sb4, func=AF.Abs, bias=bdt[:, o:o + 1]))
            p.do("scalar", lambda e: e.activation(out=ta, in_=ta, func=AF.Exp,
                                                  scale=-1.0))
            p.do("scalar", lambda e: e.activation(out=ta, in_=ta, func=AF.Ln,
                                                  bias=dt["fb"][:, 1:2]), S)
            p.fence("scalar")
            p.wait("vector", S, S.v)
            p.do("vector", lambda e: e.tensor_tensor(out=dtt, in0=trl, in1=ta,
                                                     op=ALU.add), EV)
            p.fence("vector")
            p.wait("gpsimd", EV, EV.v)
            p.dma("gpsimd", D["dtD"][o * 128:(o + 1) * 128,
                                     tt * 512:(tt + 1) * 512], dtt, ST)
    p.barrier()
    al.reset(c2mark)
    if upto <= 3:
        al.reset(base)
        return

    acm = al.f32(256)
    p.dma("sync", acm, W["AcmP"], s["WD"])
    p.wait("scalar", s["WD"], s["WD"].v)
    p.do("scalar", lambda e: e.activation(out=acm, in_=acm, func=AF.Exp), S)
    p.wait("vector", S, S.v)
    p.do("vector", lambda e: e.tensor_scalar(out=acm, in0=acm, scalar1=-1.0,
         scalar2=None, op0=ALU.mult, op1=ALU.bypass), V)
    vA = V.v

    dtc = al.bf16(T)
    xcc = al.bf16(T)
    ub = al.bf16(T)
    aarg = al.bf16(T)
    dA = al.f32(T)
    bterm = al.bf16(T)
    sout = al.bf16(T)
    yac = al.f32(T)
    Bb = [al.bf16(T), al.bf16(T)]
    zb = al.bf16(T)
    tpost = aarg
    youtb = bterm
    bb_hist = {}
    for dc in range(16):
        p.wait("sync", V, V.v)
        p.dma("sync", dtc, D["dtD"][dc * 128:(dc + 1) * 128, :], LD)
        p.dma("sync", xcc, D["xconv"][dc * 128:(dc + 1) * 128, :], LD)
        p.wait("vector", LD, LD.v)
        p.wait("vector", ST, ST.v)
        if dc == 0:
            p.wait("vector", V, vA)
        p.do("vector", lambda e: e.tensor_tensor(out=ub, in0=dtc, in1=xcc,
                                                 op=ALU.mult), V)
        for n in range(16):
            m = dc * 16 + n
            if m >= 2:
                p.wait("sync", V, bb_hist[m - 2])
            p.dma("sync", Bb[n % 2],
                  D["xssmb"][16 + n:17 + n, :].broadcast_to([128, T]), LD)
            p.do("vector", lambda e, dc=dc, n=n: e.tensor_scalar(
                out=aarg, in0=dtc, scalar1=acm[:, dc * 16 + n:dc * 16 + n + 1],
                scalar2=None, op0=ALU.mult, op1=ALU.bypass), V)
            p.wait("scalar", V, V.v)
            p.do("scalar", lambda e: e.activation(out=dA, in_=aarg, func=AF.Exp), S)
            p.fence("scalar")
            p.wait("vector", LD, LD.v)
            p.do("vector", lambda e, n=n: e.tensor_tensor(
                out=bterm, in0=ub, in1=Bb[n % 2], op=ALU.mult), V)
            bb_hist[m] = V.v
            p.wait("vector", S, S.v)
            p.do("vector", lambda e: e.tensor_tensor_scan(
                out=sout, data0=dA, data1=bterm, initial=0.0,
                op0=ALU.mult, op1=ALU.add))
            if n == 0:
                p.do("vector", lambda e: e.tensor_copy(out=yac, in_=sout), V)
            else:
                p.do("vector", lambda e: e.tensor_tensor(
                    out=yac, in0=yac, in1=sout, op=ALU.add), V)
        p.dma("sync", zb, D["xz"][(16 + dc) * 128:(17 + dc) * 128, :], LD)
        p.wait("scalar", LD, LD.v)
        p.do("scalar", lambda e: e.activation(out=zb, in_=zb, func=AF.Silu), S)
        p.fence("scalar")
        p.do("vector", lambda e, dc=dc: e.tensor_scalar(
            out=tpost, in0=xcc, scalar1=Dvec[:, dc:dc + 1], scalar2=None,
            op0=ALU.mult, op1=ALU.bypass))
        p.do("vector", lambda e: e.tensor_tensor(out=yac, in0=yac, in1=tpost,
                                                 op=ALU.add))
        p.wait("vector", S, S.v)
        p.do("vector", lambda e: e.tensor_tensor(out=youtb, in0=yac, in1=zb,
                                                 op=ALU.mult), V)
        p.fence("vector")
        p.wait("gpsimd", V, V.v)
        p.dma("gpsimd", D["yfin"][dc * 128:(dc + 1) * 128, :], youtb, ST)
    p.barrier()
    al.reset(base)
    if upto <= 4:
        return

    for half in range(2):
        hb = al.mark()
        col0 = half * 2048
        yt = [al.bf16(2048) for _ in range(16)]
        for k in range(16):
            p.dma("sync", yt[k], D["yfin"][k * 128:(k + 1) * 128, col0:col0 + 2048], LD)
        p.wait("tensor", LD, LD.v)
        xres = [al.f32(512) for _ in range(2)]
        outt = [al.f32(512) for _ in range(2)]
        tmpo = al.f32(512)
        st3 = {"n": 0}

        def evac_o(o, banks, p):
            for j, (b, tt) in enumerate(banks):
                sl = slice(col0 + tt * 512, col0 + (tt + 1) * 512)
                i = st3["n"] % 2
                p.wait("gpsimd", TM, TM.v)
                if st3["n"] >= 2:
                    p.wait("gpsimd", ST, ST.v)
                p.dma("gpsimd", xres[i], Din[o * 128:(o + 1) * 128, sl], LD)
                p.wait("vector", TM, TM.v)
                p.wait("vector", LD, LD.v)
                p.wait("vector", ST, ST.v)
                p.do("vector", lambda e, b=b: e.tensor_copy(out=tmpo, in_=b))
                p.do("vector", lambda e, i=i: e.tensor_tensor(
                    out=outt[i], in0=tmpo, in1=xres[i], op=ALU.add),
                    EV if j == len(banks) - 1 else V)
                p.fence("vector")
                p.wait("gpsimd", EV if j == len(banks) - 1 else V)
                p.wait("gpsimd", V, V.v)
                p.dma("gpsimd", Dout[o * 128:(o + 1) * 128, sl], outt[i], ST)
                st3["n"] += 1
        emit_linear(p, W["WoutB"], 16, NC8, yt, 2048, evac_o, wdtype=BF16,
                    group_barrier=True)
        al.reset(hb)
        p.barrier()
    al.reset(base)


SMALL_PARAMS = {
    "muR": [128, 8], "muK": [128, 8],
    "s1": [128, 32], "s2": [128, 32],
    "sh1": [128, 32], "sh2": [128, 32], "s3": [128, 8],
    "convW": [128, 64], "convB": [128, 16], "bdt": [128, 16], "Dvec": [128, 16],
    "dwcW": [128, 24], "dwcB": [128, 8], "qkvB": [128, 24], "projB": [128, 8],
}
STAGE_PARAMS = {"B": ("muR", "muK"), "D": ("s1", "s2", "sh1", "sh2", "s3"),
                "A": ("dwcW", "dwcB", "qkvB", "projB"),
                "C": ("convW", "convB", "bdt", "Dvec")}
BOUNDARY = {"A": ("x0T", "x1T"), "B": ("x1T", "x2T"),
            "C": ("x2T", "x3T"), "D": ("x3T", "x4T")}


def build_program(stages, io_kind="ext"):
    nc = bass.Bass("TRN2", target_bir_lowering=False)
    dram = {}
    innames = []

    def din(name, shape, dtype=F32):
        dram[name] = nc.dram_tensor(name, shape, dtype, kind="ExternalInput").ap()
        innames.append(name)
        return dram[name]

    def dout(name, shape, dtype=F32):
        dram[name] = nc.dram_tensor(name, shape, dtype, kind="ExternalOutput").ap()
        return dram[name]

    def dscr(name, shape, dtype=F32):
        dram[name] = nc.dram_tensor(name, shape, dtype, kind="Internal").ap()
        return dram[name]

    last = stages[-1]
    with contextlib.ExitStack() as stack:
        dt = {}
        dt["arena"] = stack.enter_context(nc.sbuf_tensor("arena", [128, ARENA], F32)).ap()
        dt["pb"] = [stack.enter_context(nc.psum_tensor(f"pb{i}", [128, 512], F32)).ap()
                    for i in range(8)]
        rowpool = stack.enter_context(nc.sbuf_tensor("rowpool", [1, 12288], F32)).ap()
        rows = Rows(rowpool, 12288)
        for nm, n in (("rowA", 2048), ("rowB", 2048), ("rowC", 2048),
                      ("rowE", 4096), ("rowF", 2048)):
            dt[nm] = rows.take(n)
        rowbpool = stack.enter_context(nc.sbuf_tensor("rowbpool", [1, 8192], BF16)).ap()
        dt["rowbA"] = rowbpool[0:1, 0:2048]
        dt["rowbB"] = rowbpool[0:1, 2048:4096]
        dt["rowbE"] = rowbpool[0:1, 4096:8192]
        dt["scr11"] = stack.enter_context(nc.sbuf_tensor("scr11", [1, 8], F32)).ap()
        dt["onesb"] = stack.enter_context(nc.sbuf_tensor("onesb", [128, 128], BF16)).ap()
        dt["identb"] = stack.enter_context(nc.sbuf_tensor("identb", [128, 128], BF16)).ap()
        dt["fb"] = stack.enter_context(nc.sbuf_tensor("fb", [128, 8], F32)).ap()
        for st in stages:
            for nm in STAGE_PARAMS[st]:
                dt[nm] = stack.enter_context(
                    nc.sbuf_tensor("p_" + nm, SMALL_PARAMS[nm], F32)).ap()
        sems = {n: Sem(stack.enter_context(nc.semaphore(n)), n) for n in SEM_NAMES}
        p = Prog(nc, sems, dt)
        al = Alloc(dt["arena"], ARENA)
        dt["wstg"] = [al.f32(1024), al.f32(1024)]
        dt["wsl"] = [al.bf16(4096), al.bf16(4096)]
        dt["lin"] = {"n": 0, "g": 0, "cast": {}, "tm": {}}

        din("cst_onesb", [128, 128], BF16)
        din("cst_identb", [128, 128], BF16)
        din("cst_fb", [128, 8], F32)
        for st in stages:
            for nm in STAGE_PARAMS[st]:
                din("cst_" + nm, SMALL_PARAMS[nm])
        for i, st in enumerate(stages):
            nin, nout = BOUNDARY[st]
            if nin not in dram:
                din(nin, [C, T])
            if nout not in dram:
                if st == last or io_kind != "fused":
                    dout(nout, [C, T])
                else:
                    dscr(nout, [C, T])
        W = {}
        Dc = {}
        Da = {}
        if "A" in stages:
            W["qkvS"] = din("qkvS", [24 * 128, 1024])
            W["projS"] = din("projS", [8 * 128, 1024])
            dmka = dscr if io_kind == "fused" else (lambda n, s, d=F32: dout(n, s, d))
            Da["qkT"] = dmka("a_qkT", [3072, T], BF16)
            Da["ktokD"] = dmka("a_ktok", [T, 1024], BF16)
            Da["vtokD"] = dmka("a_vtok", [T, 1024], BF16)
            Da["localD"] = dmka("a_local", [1024, T], BF16)
            Da["yTD"] = dmka("a_yT", [1024, T], BF16)
            Da["rbrow"] = dscr("a_rbrow", [1, T], BF16)
        if "B" in stages:
            for nm in ("WrS", "WkS", "WvS"):
                W[nm] = din(nm, [8 * 128, 1024])
        if "C" in stages:
            W["WinS"] = din("WinS", [32 * 128, 1024])
            W["WoutS"] = din("WoutS", [8 * 128, 2048])
            W["WoutB"] = dscr("WoutB", [8 * 128, 2048], BF16)
            W["WxP"] = din("WxP", [128, 512])
            W["WdtP"] = din("WdtP", [16, 2048])
            W["AcmP"] = din("AcmP", [128, 256])
            dmk = dscr if io_kind == "fused" else (lambda n, s, d=F32: dout(n, s, d))
            Dc["xz"] = dmk("c_xz", [4096, T], BF16)
            Dc["xconv"] = dmk("c_xconv", [2048, T], BF16)
            Dc["xssmb"] = dmk("c_xssmb", [32, T], BF16)
            Dc["dtD"] = dmk("c_dt", [2048, T], BF16)
            Dc["yfin"] = dmk("c_yfin", [2048, T], BF16)
        if "D" in stages:
            W["w1S"] = din("w1S", [32 * 128, 1024])
            W["w2S"] = din("w2S", [32 * 128, 1024])
            W["h1S"] = din("h1S", [32 * 128, 4096])
            W["h2S"] = din("h2S", [32 * 128, 4096])
            W["w3S"] = din("w3S", [8 * 128, 4096])
            dmkd = dscr if io_kind == "fused" else (lambda n, s, d=F32: dout(n, s, d))
            W["h1B"] = dmkd("h1B", [32 * 128, 4096], BF16)
            W["h2B"] = dscr("h2B", [32 * 128, 4096], BF16)
            W["w3B"] = dscr("w3B", [8 * 128, 4096], BF16)
            dmkd("hSpill", [4096, T], BF16)
            dmkd("hSpill2", [4096, T], BF16)

        s = p.sems
        for nm in sorted(dram):
            if nm.startswith("cst_"):
                p.dma("sync", dt[nm[4:]], dram[nm], s["LD"])
        for e in ("vector", "scalar", "tensor", "gpsimd"):
            p.wait(e, s["LD"], s["LD"].v)

        for st in stages:
            nin, nout = BOUNDARY[st]
            if st == "A":
                import os
                emit_stage_A(p, al, dram[nin], W, Da, dram[nout],
                             heads=int(os.environ.get("A_HEADS", "16")),
                             upto=int(os.environ.get("A_UPTO", "99")))
            elif st == "B":
                emit_stage_B(p, al, dram[nin], W, dram[nout])
            elif st == "C":
                import os
                emit_stage_C(p, al, dram[nin], W, Dc, dram[nout],
                             upto=int(os.environ.get("C_UPTO", "99")))
            elif st == "D":
                emit_stage_D(p, al, dram[nin], W, dram["hSpill"],
                             dram["hSpill2"], dram[nout])
            else:
                raise NotImplementedError(st)
            p.barrier()

        p.wait("gpsimd", s["ST"], s["ST"].v)
        with nc.Block() as block:
            p.emit(block)
    return nc, dram, innames


def prep_const_inputs():
    fb = np.zeros((128, 8), np.float32)
    fb[:, 0] = 1e-5
    fb[:, 1] = 1.0
    fb[:, 2] = 1e-6
    fb[:, 3] = float(np.log(1e-6))
    return {"cst_onesb": np.ones((128, 128), BF),
            "cst_identb": np.eye(128, dtype=BF),
            "cst_fb": fb}


def prep_B_inputs(inp):
    f = lambda k: np.asarray(inp[k], np.float32)
    return {
        "WrS": slabify(f("L0_ffn_W_r").T),
        "WkS": slabify(f("L0_ffn_W_k").T),
        "WvS": slabify(f("L0_ffn_W_v").T),
        "cst_muR": colmajor(f("L0_ffn_mu_r")),
        "cst_muK": colmajor(f("L0_ffn_mu_k")),
    }


def prep_A_inputs(inp):
    f = lambda k: np.asarray(inp[k], np.float32)
    return {
        "qkvS": slabify(f("L0_attn_qkv_w").T),
        "projS": slabify(f("L0_attn_proj_w").T),
        "cst_qkvB": colmajor(f("L0_attn_qkv_b")),
        "cst_projB": colmajor(f("L0_attn_proj_b")),
        "cst_dwcW": np.ascontiguousarray(
            f("L0_attn_dwc_w")[:, 0, :].reshape(8, 128, 3).transpose(1, 0, 2).reshape(128, 24)),
        "cst_dwcB": colmajor(f("L0_attn_dwc_b")),
    }


def prep_C_inputs(inp):
    f = lambda k: np.asarray(inp[k], np.float32)
    return {
        "WinS": slabify(f("L1_attn_W_in")),
        "WoutS": slabify(f("L1_attn_W_out")),
        "WxP": np.ascontiguousarray(
            f("L1_attn_W_x").reshape(16, 128, 32).transpose(1, 0, 2).reshape(128, 512)),
        "WdtP": np.ascontiguousarray(f("L1_attn_W_dt")),
        "AcmP": np.ascontiguousarray(
            f("L1_attn_A_log").reshape(16, 128, 16).transpose(1, 0, 2).reshape(128, 256)),
        "cst_convW": np.ascontiguousarray(
            f("L1_attn_conv_w").reshape(16, 128, 4).transpose(1, 0, 2).reshape(128, 64)),
        "cst_convB": colmajor(f("L1_attn_conv_b")),
        "cst_bdt": colmajor(f("L1_attn_b_dt")),
        "cst_Dvec": colmajor(f("L1_attn_D")),
    }


def prep_D_inputs(inp):
    f = lambda k: np.asarray(inp[k], np.float32)
    return {
        "w1S": slabify(f("L1_ffn_w1_w").T),
        "w2S": slabify(f("L1_ffn_w2_w").T),
        "h1S": slabify(f("L1_ffn_h1_w").T),
        "h2S": slabify(f("L1_ffn_h2_w").T),
        "w3S": slabify(f("L1_ffn_w3_w").T),
        "cst_s1": colmajor(f("L1_ffn_w1_s")),
        "cst_s2": colmajor(f("L1_ffn_w2_s")),
        "cst_sh1": colmajor(f("L1_ffn_h1_s")),
        "cst_sh2": colmajor(f("L1_ffn_h2_s")),
        "cst_s3": colmajor(f("L1_ffn_w3_s")),
    }


FUSED = True


def _all_prepped(inputs):
    base = dict(prep_const_inputs())
    base.update(prep_A_inputs(inputs))
    base.update(prep_B_inputs(inputs))
    base.update(prep_C_inputs(inputs))
    base.update(prep_D_inputs(inputs))
    return base


def kernel(**inputs):
    x = np.asarray(inputs["L0_attn_x"], np.float32)
    base = _all_prepped(inputs)
    xT = [np.ascontiguousarray(x[b].T) for b in range(8)]
    if FUSED:
        nc, dram, innames = build_program(("A", "B", "C", "D"), io_kind="fused")
        im = [{k: (xT[b] if k == "x0T" else base[k]) for k in innames}
              for b in range(8)]
        res = run_bass_kernel_spmd(nc, im, core_ids=list(range(8)))
        return np.stack([np.asarray(res.results[b]["x4T"], np.float32).T
                         for b in range(8)])
    cur = {"x0T": xT}
    for st in ("A", "B", "C", "D"):
        nin, nout = BOUNDARY[st]
        nc, dram, innames = build_program((st,), io_kind="fused")
        im = [{k: (cur[nin][b] if k == nin else base[k]) for k in innames}
              for b in range(8)]
        res = run_bass_kernel_spmd(nc, im, core_ids=list(range(8)))
        cur[nout] = [np.ascontiguousarray(np.asarray(res.results[b][nout],
                                                     np.float32))
                     for b in range(8)]
    return np.stack([cur["x4T"][b].T for b in range(8)])
